# version 03
# baseline (speedup 1.0000x reference)
import numpy as np
import ml_dtypes
import concourse.bass as bass
import concourse.mybir as mybir
from concourse.bass_utils import run_bass_kernel_spmd

F32 = mybir.dt.float32
BF16 = mybir.dt.bfloat16
AF = mybir.ActivationFunctionType
ALU = mybir.AluOpType
AX = mybir.AxisListType

T = 2048
D = 2048
NT = T // 128
ND = D // 128
NEG = -1.0e30

WINDOW = (0, 8)

ENGINES = ("sy", "gp", "ve", "sc", "te")
_EMETH = {"sy": "sync", "gp": "gpsimd", "ve": "vector", "sc": "scalar", "te": "tensor"}
SBUF_BASE = 20480
SBUF_END = 229376


class KB:

    def __init__(self, nc):
        self.nc = nc
        self.ops = {e: [] for e in ENGINES}
        self.cnt = {e: 0 for e in ENGINES}
        self.wr = {}
        self.rd = {}
        self.last_wait = {e: {} for e in ENGINES}
        self.fence_w = {e: {} for e in ENGINES}
        self.sbuf_off = SBUF_BASE

    def fence(self):
        snap = {e: v for e, v in self.cnt.items() if v}
        for e in ENGINES:
            d = self.fence_w[e]
            for pe, pv in snap.items():
                if pe != e and d.get(pe, 0) < pv:
                    d[pe] = pv

    def salloc(self, name, shape, dtype, align=64):
        nbytes = int(np.prod(shape[1:])) * mybir.dt.size(dtype)
        off = (self.sbuf_off + align - 1) // align * align
        t = self.nc.alloc_sbuf_tensor_at(name, list(shape), dtype, offset=off)
        self.sbuf_off = off + nbytes
        assert self.sbuf_off <= SBUF_END, f"SBUF overflow at {name}: {self.sbuf_off}"
        return t.ap()

    def mark(self):
        return self.sbuf_off

    def release(self, mk):
        self.sbuf_off = mk

    def op(self, eng, meth, *args, r=(), w=(), dma=False, **kw):
        waits = {}
        fw = self.fence_w[eng]
        if fw:
            for pe, pv in fw.items():
                if self.last_wait[eng].get(pe, 0) < pv and waits.get(pe, 0) < pv:
                    waits[pe] = pv
            self.fence_w[eng] = {}

        def need(tok):
            if tok is None:
                return
            pe, pv = tok
            if pe == eng and eng == "te":
                return
            if self.last_wait[eng].get(pe, 0) >= pv:
                return
            if waits.get(pe, 0) < pv:
                waits[pe] = pv

        for reg in r:
            need(self.wr.get(reg))
        for reg in w:
            need(self.wr.get(reg))
            for tok in self.rd.get(reg, ()):
                need(tok)
        inc = 16 if dma else 1
        self.cnt[eng] += inc
        tok = (eng, self.cnt[eng])
        for pe, pv in waits.items():
            self.last_wait[eng][pe] = pv
        self.ops[eng].append((meth, args, kw, sorted(waits.items()), inc))
        for reg in r:
            self.rd.setdefault(reg, []).append(tok)
        for reg in w:
            self.wr[reg] = tok
            self.rd[reg] = []
        return tok

    def ld(self, dst, src, r=(), w=()):
        return self.op("sy", "dma_start", r=r, w=w, dma=True, out=dst, in_=src)

    def st(self, dst, src, r=(), w=()):
        return self.op("gp", "dma_start", r=r, w=w, dma=True, out=dst, in_=src)

    def v(self, meth, *a, r=(), w=(), **kw):
        return self.op("ve", meth, *a, r=r, w=w, **kw)

    def s(self, meth, *a, r=(), w=(), **kw):
        return self.op("sc", meth, *a, r=r, w=w, **kw)

    def t(self, meth, *a, r=(), w=(), **kw):
        return self.op("te", meth, *a, r=r, w=w, **kw)

    def emit(self):
        nc = self.nc
        from contextlib import ExitStack
        with ExitStack() as es:
            sems = {e: es.enter_context(nc.semaphore(f"sem_{e}")) for e in ENGINES}
            block = es.enter_context(nc.Block())
            for e in ENGINES:
                def body(engine, e=e):
                    for meth, args, kw, waits, inc in self.ops[e]:
                        for pe, pv in waits:
                            engine.wait_ge(sems[pe], pv)
                        getattr(engine, meth)(*args, **kw).then_inc(sems[e], inc)
                    if e == "sy":
                        for pe in ENGINES:
                            if self.cnt[pe]:
                                engine.wait_ge(sems[pe], self.cnt[pe])
                getattr(block, _EMETH[e])(body)



def _cm(wt):
    K, N = wt.shape
    assert K % 128 == 0
    return np.ascontiguousarray(
        wt.reshape(K // 128, 128, N).transpose(1, 0, 2).reshape(128, (K // 128) * N))


def _rows(v):
    return np.ascontiguousarray(np.tile(np.asarray(v, np.float32)[None, :], (128, 1)))


def _cols(v):
    v = np.asarray(v, np.float32)
    return np.ascontiguousarray(v.reshape(-1, 128).T)


def host_prep(ins):
    d = {}
    f32 = lambda a: np.ascontiguousarray(np.asarray(a, np.float32))
    d["c_identb"] = np.eye(128, dtype=ml_dtypes.bfloat16)
    d["c_identf"] = np.eye(128, dtype=np.float32)
    mask = np.zeros((128, 128), np.float32)
    mask[np.triu_indices(128, 1)] = NEG
    d["c_mask"] = mask
    d["L0_WtsT"] = _cm(f32(ins["L0_attn_Wts"]).T)
    d["L0_WkvT"] = _cm(np.concatenate([f32(ins["L0_attn_Wk"]).T,
                                       f32(ins["L0_attn_Wv"]).T], axis=1))
    d["L0_WoT"] = np.ascontiguousarray(f32(ins["L0_attn_Wo"]).T)
    d["L0_gcol"] = _cols(ins["L0_attn_gate"])
    d["L0_tdc"] = f32(ins["L0_attn_time_decay"]).reshape(64, 1)
    d["L0_tfc"] = f32(ins["L0_attn_time_first"]).reshape(64, 1)
    d["L0_lnw"] = _rows(ins["L0_attn_ln_w"])
    d["L0_lnb"] = _rows(ins["L0_attn_ln_b"])
    d["L0_fW"] = _cm(f32(ins["L0_ffn_weight"]))
    d["L0_fWT"] = _cm(f32(ins["L0_ffn_weight"]).T)
    d["L0_fbr"] = _rows(ins["L0_ffn_bias"])
    d["L1_CQwT"] = _cm(f32(ins["L1_attn_CQ_w"]).T)
    d["L1_CQb"] = _cols(ins["L1_attn_CQ_b"])
    d["L1_CQCwT"] = _cm(f32(ins["L1_attn_CQC_w"]).T)
    d["L1_CQCb"] = _cols(ins["L1_attn_CQC_b"])
    d["L1_CQRwT"] = _cm(f32(ins["L1_attn_CQR_w"]).T)
    d["L1_CQRb"] = _cols(ins["L1_attn_CQR_b"])
    d["L1_CKVwT"] = _cm(f32(ins["L1_attn_CKV_w"]).T)
    d["L1_CKVb"] = _cols(ins["L1_attn_CKV_b"])
    d["L1_CKRwT"] = _cm(f32(ins["L1_attn_CKR_w"]).T)
    d["L1_CKRb"] = _cols(ins["L1_attn_CKR_b"])
    d["L1_CKCwT"] = _cm(f32(ins["L1_attn_CKC_w"]).T)
    d["L1_CKCb"] = _cols(ins["L1_attn_CKC_b"])
    d["L1_CVwT"] = _cm(f32(ins["L1_attn_CV_w"]).T)
    d["L1_CVbr"] = _rows(ins["L1_attn_CV_b"])
    d["L1_OUTwT"] = _cm(f32(ins["L1_attn_OUT_w"]).T)
    d["L1_OUTbr"] = _rows(ins["L1_attn_OUT_b"])
    d["L1_fcWT"] = _cm(f32(ins["L1_ffn_w_fc"]).T)
    d["L1_pjWT"] = _cm(f32(ins["L1_ffn_w_proj"]).T)
    i = np.arange(0, 128, 2, dtype=np.float64) / 128.0
    inv = 1.0 / (10000.0 ** i)
    fr = np.concatenate([np.outer(inv, np.arange(T, dtype=np.float64))] * 2, 0)
    sin_t = np.sin(fr)
    sin_t[0:64, :] *= -1.0
    for nm, tab in (("cos", np.cos(fr)), ("sin", sin_t)):
        hi = tab.astype(np.float32)
        lo = (tab - hi.astype(np.float64)).astype(np.float32)
        d[f"L1_{nm}h"] = hi
        d[f"L1_{nm}l"] = lo
    d["L2_qkvT"] = _cm(f32(ins["L2_attn_w_qkv"]).T)
    d["L2_qkb"] = _cols(ins["L2_attn_b_qkv"])
    d["L2_outT"] = _cm(f32(ins["L2_attn_w_out"]).T)
    d["L2_obr"] = _rows(ins["L2_attn_b_out"])
    d["L2_fW"] = _cm(f32(ins["L2_ffn_weight"]))
    d["L2_fsr"] = _rows(ins["L2_ffn_scale"])
    d["L3_WqT"] = _cm(f32(ins["L3_attn_Wq"]).T)
    d["L3_WkT"] = _cm(f32(ins["L3_attn_Wk"]).T)
    d["L3_WvT"] = _cm(f32(ins["L3_attn_Wv"]).T)
    d["L3_WoT"] = _cm(f32(ins["L3_attn_Wo"]).T)
    d["L3_fcWT"] = _cm(f32(ins["L3_ffn_w_fc"]).T)
    d["L3_pjWT"] = _cm(f32(ins["L3_ffn_w_proj"]).T)
    return d


_SHAPES = {
    "c_identb": ([128, 128], BF16), "c_identf": ([128, 128], F32),
    "c_mask": ([128, 128], F32),
    "L0_WtsT": ([128, ND * D], F32), "L0_WkvT": ([128, ND * 128], F32),
    "L0_WoT": ([64, D], F32), "L0_gcol": ([128, 16], F32),
    "L0_tdc": ([64, 1], F32), "L0_tfc": ([64, 1], F32),
    "L0_lnw": ([128, D], F32), "L0_lnb": ([128, D], F32),
    "L0_fW": ([128, ND * D], F32), "L0_fWT": ([128, ND * D], F32),
    "L0_fbr": ([128, D], F32),
    "L1_CQwT": ([128, 16 * 512], F32), "L1_CQb": ([128, 4], F32),
    "L1_CQCwT": ([128, 4 * 384], F32), "L1_CQCb": ([128, 3], F32),
    "L1_CQRwT": ([128, 16 * 128], F32), "L1_CQRb": ([128, 1], F32),
    "L1_CKVwT": ([128, 16 * 512], F32), "L1_CKVb": ([128, 4], F32),
    "L1_CKRwT": ([128, 16 * 128], F32), "L1_CKRb": ([128, 1], F32),
    "L1_CKCwT": ([128, 4 * 384], F32), "L1_CKCb": ([128, 3], F32),
    "L1_CVwT": ([128, 4 * 512], F32), "L1_CVbr": ([128, 512], F32),
    "L1_OUTwT": ([128, 4 * 2048], F32), "L1_OUTbr": ([128, D], F32),
    "L1_fcWT": ([128, 16 * 5120], F32), "L1_pjWT": ([128, 40 * 2048], F32),
    "L1_cosh": ([128, T], F32), "L1_cosl": ([128, T], F32),
    "L1_sinh": ([128, T], F32), "L1_sinl": ([128, T], F32),
    "L2_qkvT": ([128, 16 * 6144], F32), "L2_qkb": ([128, 48], F32),
    "L2_outT": ([128, ND * D], F32), "L2_obr": ([128, D], F32),
    "L2_fW": ([128, ND * D], F32), "L2_fsr": ([128, D], F32),
    "L3_WqT": ([128, ND * D], F32), "L3_WkT": ([128, ND * D], F32),
    "L3_WvT": ([128, ND * D], F32), "L3_WoT": ([128, ND * D], F32),
    "L3_fcWT": ([128, 16 * 8192], F32), "L3_pjWT": ([128, 64 * 2048], F32),
}


def cm3(ap, nch):
    return ap.rearrange("p (c n) -> p c n", c=nch)


class Prog:
    def __init__(self):
        nc = bass.Bass("TRN2", target_bir_lowering=False, debug=False)
        self.nc = nc
        self.K = KB(nc)
        self.din = {}
        for name, (shape, dt) in _SHAPES.items():
            self.din[name] = nc.dram_tensor(name, shape, dt, kind="ExternalInput").ap()
        self.x_in = nc.dram_tensor("x", [T, D], F32, kind="ExternalInput").ap()
        self.y = nc.dram_tensor("y", [T, D], F32, kind="ExternalOutput").ap()
        self.xres = nc.dram_tensor("xres", [T, D], F32, kind="Internal").ap()
        self.scrX = nc.dram_tensor("scrX", [128, 2 * ND * T], BF16, kind="Internal").ap()
        self.scrX2 = nc.dram_tensor("scrX2", [128, 2 * ND * T], BF16, kind="Internal").ap()
        self.scrA = nc.dram_tensor("scrA", [128, 16 * 8192], BF16, kind="Internal").ap()
        self.scrB = nc.dram_tensor("scrB", [128, 64 * 2048], BF16, kind="Internal").ap()
        self.scrD = nc.dram_tensor("scrD", [128, 64 * 8192], BF16, kind="Internal").ap()

        K = self.K
        self.PS = nc.alloc_psum_tensor("PS", [128, 3072], F32).ap()
        self.PT6 = nc.alloc_psum_tensor("PT6", [128, 1024], BF16).ap()
        self.PT7 = nc.alloc_psum_tensor("PT7", [128, 1024], BF16).ap()
        self.identb = K.salloc("identb", [128, 128], BF16)
        self.identf = K.salloc("identf", [128, 128], F32)
        self.mask = K.salloc("mask", [128, 128], F32)
        self.epsc = K.salloc("epsc", [128, 4], F32)
        self.onec = K.salloc("onec", [128, 1], F32)
        self.oner = K.salloc("oner", [1, 128], F32)
        self.stats = K.salloc("stats", [128, 12], F32)
        self.xt = K.salloc("g_xt", [128, D], F32)
        self.xn = K.salloc("g_xn", [128, D], F32)
        self.sq = K.salloc("g_sq", [128, D], BF16)
        self.hh = K.salloc("g_hh", [128, D], BF16)
        self.hl = K.salloc("g_hl", [128, D], BF16)
        self.ot = K.salloc("g_ot", [128, D], F32)
        self.ot2 = K.salloc("g_ot2", [128, D], F32)
        self.sfm = K.salloc("g_sfm", [128, D], BF16)
        self.sfm2 = K.salloc("g_sfm2", [128, D], BF16)
        self.tcol = K.salloc("g_tcol", [128, 8], F32)
        self.t11 = K.salloc("g_t11", [1, 8], F32)
        self.glob_end = K.mark()

        K.ld(self.identb[:, :], self.din["c_identb"][:, :], w=["identb"])
        K.ld(self.identf[:, :], self.din["c_identf"][:, :], w=["identf"])
        K.ld(self.mask[:, :], self.din["c_mask"][:, :], w=["mask"])
        for j, val in enumerate((1e-5, 1e-6, 1e-7, 1e-8)):
            K.v("memset", self.epsc[:, j:j + 1], val, w=["epsc"])
        K.v("memset", self.onec[:, :], 1.0, w=["onec"])
        K.v("memset", self.oner[:, :], 1.0, w=["oner"])

    def ln_cols(self, s1c, s2c, mcol, rcol, dfeat, eps):
        K, st = self.K, self.stats
        K.v("tensor_scalar", mcol, s1c, 1.0 / dfeat, None, ALU.mult, r=["st"], w=["st"])
        K.v("tensor_tensor", st[:, 0:1], mcol, mcol, ALU.mult, r=["st"], w=["st"])
        K.v("scalar_tensor_tensor", st[:, 1:2], s2c, 1.0 / dfeat, st[:, 0:1],
            ALU.mult, ALU.subtract, r=["st"], w=["st"])
        K.s("activation", st[:, 0:1], st[:, 1:2], AF.Sqrt, bias=eps,
            r=["st", "epsc"], w=["st"])
        K.v("reciprocal", rcol, st[:, 0:1], r=["st"], w=["st"])

    def ln_apply(self, src, dst, dfeat, eps, r_src, w_dst, sq=None, w_sq=None):
        K, st = self.K, self.stats
        sq = sq if sq is not None else self.sq[:, 0:dfeat]
        w_sq = w_sq or "g_sq"
        K.v("reduce_sum", st[:, 2:3], src, axis=AX.X, r=[r_src], w=["st"])
        K.s("activation", sq, src, AF.Square, accum_out=st[:, 3:4],
            r=[r_src], w=["st", w_sq])
        self.ln_cols(st[:, 2:3], st[:, 3:4], st[:, 4:5], st[:, 5:6], dfeat, eps)
        K.v("scalar_tensor_tensor", dst, src, st[:, 4:5],
            st[:, 5:6].broadcast_to((128, dfeat)), ALU.subtract, ALU.mult,
            r=[r_src, "st"], w=[w_dst])

    def split(self, src, hi, lo, r_src, w_hi, w_lo):
        K = self.K
        K.v("tensor_copy", hi, src, r=[r_src], w=[w_hi])
        K.v("tensor_tensor", lo, src, hi, ALU.subtract, r=[r_src, w_hi], w=[w_lo])

    def tr_blocks(self, src, nblk, dstcol, dst, r_src, w_dst, bank=6, evac="ve"):
        K = self.K
        PT = self.PT6 if bank == 6 else self.PT7
        rps = f"pt{bank}"
        for c0 in range(0, nblk, 4):
            nb = min(4, nblk - c0)
            for j in range(nb):
                c = c0 + j
                K.t("transpose", PT[:, 128 * j:128 * (j + 1)],
                    src[:, 128 * c:128 * (c + 1)], self.identb[:, :],
                    r=[r_src, "identb"], w=[rps])
            for j in range(nb):
                c = c0 + j
                if evac == "ve":
                    K.v("tensor_copy", dst[:, dstcol(c):dstcol(c) + 128],
                        PT[:, 128 * j:128 * (j + 1)], r=[rps], w=[w_dst])
                else:
                    K.s("copy", dst[:, dstcol(c):dstcol(c) + 128],
                        PT[:, 128 * j:128 * (j + 1)], r=[rps], w=[w_dst])

    def stage_xT(self, x_src, scr, rscr, extra=None, nch=ND):
        K = self.K
        for t in range(NT):
            K.ld(self.xt, x_src[128 * t:128 * (t + 1), :], r=["xres"], w=["g_xt"])
            self.ln_apply(self.xt, self.xn, D, self.epsc[:, 0:1], "g_xt", "g_xn")
            if extra is not None:
                extra(t)
            self.split(self.xn, self.hh, self.hl, "g_xn", "g_hh", "g_hl")
            self.tr_blocks(self.hh, nch, lambda c: 128 * c, self.sfm, "g_hh",
                           "g_sfm", bank=6)
            self.tr_blocks(self.hl, nch, lambda c: 128 * c, self.sfm2, "g_hl",
                           "g_sfm2", bank=7)
            K.st(cm3(scr[:, 0:nch * T], nch)[:, :, 128 * t:128 * (t + 1)],
                 cm3(self.sfm[:, 0:nch * 128], nch), r=["g_sfm"], w=[rscr])
            K.st(cm3(scr[:, nch * T:2 * nch * T], nch)[:, :, 128 * t:128 * (t + 1)],
                 cm3(self.sfm2[:, 0:nch * 128], nch), r=["g_sfm2"], w=[rscr])

    def ld_slice(self, scr, rscr, tc, bufh, bufl, wtag, nch=ND, width=512):
        K = self.K
        off = width * tc
        K.ld(cm3(bufh[:, 0:nch * width], nch),
             cm3(scr[:, 0:nch * T], nch)[:, :, off:off + width], r=[rscr],
             w=[wtag + "h"])
        K.ld(cm3(bufl[:, 0:nch * width], nch),
             cm3(scr[:, nch * T:2 * nch * T], nch)[:, :, off:off + width],
             r=[rscr], w=[wtag + "l"])

    def finish_tile(self, t, src, r_src, x_src, x_dst, bias_row=None, r_bias=None,
                    pre=None):
        K = self.K
        r_src = list(r_src)
        if bias_row is not None:
            K.v("tensor_tensor", self.ot, src, bias_row, ALU.add,
                r=r_src + [r_bias], w=["g_ot"])
            src, r_src = self.ot, ["g_ot"]
        if pre is not None:
            src, r_src = pre(src, r_src)
        st = self.stats
        K.v("reduce_sum", st[:, 2:3], src, axis=AX.X, r=r_src, w=["st"])
        K.s("activation", self.sq, src, AF.Square, accum_out=st[:, 3:4],
            r=r_src, w=["st", "g_sq"])
        self.ln_cols(st[:, 2:3], st[:, 3:4], st[:, 4:5], st[:, 5:6], D,
                     self.epsc[:, 0:1])
        K.v("scalar_tensor_tensor", self.ot2, src, st[:, 4:5],
            st[:, 5:6].broadcast_to((128, D)), ALU.subtract, ALU.mult,
            r=r_src + ["st"], w=["g_ot2"])
        K.ld(self.xt, x_src[128 * t:128 * (t + 1), :], r=["xres"], w=["g_xt"])
        K.v("tensor_tensor", self.ot, self.ot2, self.xt, ALU.add,
            r=["g_ot2", "g_xt"], w=["g_ot"])
        K.st(x_dst[128 * t:128 * (t + 1), :], self.ot, r=["g_ot"], w=["xres"])

    def bc_col(self, src11, r_src, dst_col, w_dst):
        K = self.K
        ps = self.PS[:, 2560:2561]
        K.t("matmul", ps, self.oner[:, :], src11, start=True, stop=True,
            r=[r_src, "oner"], w=["ps5"])
        K.v("tensor_copy", dst_col, ps, r=["ps5"], w=[w_dst])

    def colsum(self, src_col, r_src, dst11, w_dst):
        K = self.K
        ps = self.PS[0:1, 2562:2563]
        K.t("matmul", ps, src_col, self.onec[:, :], start=True, stop=True,
            r=[r_src, "onec"], w=["ps5"])
        K.v("tensor_copy", dst11, ps, r=["ps5"], w=[w_dst])

    def l0_attn(self, x_src, x_dst):
        K, din = self.K, self.din
        mk = K.mark()
        xsh = K.salloc("a0_xsh", [128, ND * 512], BF16)
        xsl = K.salloc("a0_xsl", [128, ND * 512], BF16)
        wb = K.salloc("a0_wb", [128, D], F32)
        wh = K.salloc("a0_wh", [128, D], BF16)
        wl = K.salloc("a0_wl", [128, D], BF16)
        gc = K.salloc("a0_gc", [128, 16], F32)
        g1 = K.salloc("a0_g1", [128, 16], F32)
        b1 = self.xn[:, 0:512]
        b2 = self.xn[:, 512:1024]
        b3 = self.xn[:, 1024:1536]
        lnw = K.salloc("a0_lnw", [128, D], F32)
        lnb = K.salloc("a0_lnb", [128, D], F32)
        kvh = K.salloc("a0_kvh", [128, ND * 128], BF16)
        kvl = K.salloc("a0_kvl", [128, ND * 128], BF16)
        kT = wb[0:64, :]
        vT = lnb[0:64, :]
        wkv = K.salloc("a0_wkv", [64, T], F32)
        states = K.salloc("a0_st", [64, T], F32)
        sth = K.salloc("a0_sth", [64, T], BF16)
        stl = K.salloc("a0_stl", [64, T], BF16)
        woh = K.salloc("a0_woh", [64, D], BF16)
        wol = K.salloc("a0_wol", [64, D], BF16)
        tdc = K.salloc("a0_tdc", [64, 1], F32)
        tfc = K.salloc("a0_tfc", [64, 1], F32)
        slh = K.salloc("a0_slh", [128, ND * 512], BF16)
        sll = K.salloc("a0_sll", [128, ND * 512], BF16)
        gth = slh[:, 0:ND * 128]
        gtl = sll[:, 0:ND * 128]

        K.ld(gc[:, :], din["L0_gcol"][:, :], w=["a0_gc"])
        K.s("activation", gc[:, :], gc[:, :], AF.Sigmoid, r=["a0_gc"], w=["a0_gc"])
        K.v("tensor_scalar", g1[:, :], gc[:, :], -1.0, 1.0, ALU.mult, ALU.add,
            r=["a0_gc"], w=["a0_g1"])
        K.ld(tdc[:, :], din["L0_tdc"][:, :], w=["a0_tdc"])
        K.s("activation", tdc[:, :], tdc[:, :], AF.Exp, r=["a0_tdc"], w=["a0_tdc"])
        K.ld(tfc[:, :], din["L0_tfc"][:, :], w=["a0_tfc"])
        K.s("activation", tfc[:, :], tfc[:, :], AF.Exp, r=["a0_tfc"], w=["a0_tfc"])
        K.v("tensor_scalar", tfc[:, :], tfc[:, :], -1.0, None, ALU.mult,
            r=["a0_tfc"], w=["a0_tfc"])
        K.ld(wb[:, :], din["L0_WkvT"][:, :], w=["a0_wb"])
        self.split(wb[:, :], kvh[:, :], kvl[:, :], "a0_wb", "a0_kvh", "a0_kvl")
        K.ld(wb[0:64, :], din["L0_WoT"][:, :], w=["a0_wb"])
        self.split(wb[0:64, :], woh[:, :], wol[:, :], "a0_wb", "a0_woh", "a0_wol")
        K.ld(lnw[:, :], din["L0_lnw"][:, :], w=["a0_lnw"])
        K.ld(lnb[:, :], din["L0_lnb"][:, :], w=["a0_lnb"])

        self.stage_xT(x_src, self.scrX, "scrX")

        for tc in range(4):
            rtc = ((512 * tc + 1024) % T) // 512
            self.ld_slice(self.scrX, "scrX", rtc, slh, sll, "a0_sl")
            self.ld_slice(self.scrX, "scrX", tc, xsh, xsl, "a0_xs")
            for dc in range(ND):
                src = cm3(din["L0_WtsT"][:, :], ND)[:, :, 128 * dc:128 * (dc + 1)]
                K.ld(cm3(wb[:, :], ND), src, w=["a0_wb"])
                self.split(wb[:, :], wh[:, :], wl[:, :], "a0_wb", "a0_wh", "a0_wl")
                pb = dc % 2
                ps = self.PS[:, 512 * pb:512 * (pb + 1)]
                for c in range(ND):
                    lh = wh[:, 128 * c:128 * (c + 1)]
                    ll = wl[:, 128 * c:128 * (c + 1)]
                    rh = slh[:, 512 * c:512 * (c + 1)]
                    rl = sll[:, 512 * c:512 * (c + 1)]
                    K.t("matmul", ps, lh, rh, start=(c == 0), stop=False,
                        r=["a0_wh", "a0_slh"], w=[f"ps{pb}"])
                    K.t("matmul", ps, lh, rl, start=False, stop=False,
                        r=["a0_wh", "a0_sll"], w=[f"ps{pb}"])
                    K.t("matmul", ps, ll, rh, start=False, stop=(c == ND - 1),
                        r=["a0_wl", "a0_slh"], w=[f"ps{pb}"])
                xh = xsh[:, 512 * dc:512 * (dc + 1)]
                xl = xsl[:, 512 * dc:512 * (dc + 1)]
                K.v("tensor_scalar", b1[:, :], xh, g1[:, dc:dc + 1], None, ALU.mult,
                    r=["a0_xsh", "a0_g1"], w=["a0_b1", "g_xn"])
                K.v("scalar_tensor_tensor", b2[:, :], xl, g1[:, dc:dc + 1], b1[:, :],
                    ALU.mult, ALU.add, r=["a0_xsl", "a0_g1", "a0_b1"], w=["a0_b2"])
                K.v("scalar_tensor_tensor", b3[:, :], ps, gc[:, dc:dc + 1], b2[:, :],
                    ALU.mult, ALU.add, r=[f"ps{pb}", "a0_gc", "a0_b2"], w=["a0_b3"])
                K.v("tensor_copy", self.sfm[:, 0:512], b3[:, :], r=["a0_b3"],
                    w=["g_sfm"])
                K.v("tensor_tensor", self.sfm2[:, 0:512], b3[:, :],
                    self.sfm[:, 0:512], ALU.subtract, r=["a0_b3", "g_sfm"],
                    w=["g_sfm2"])
                K.st(self.scrD[:, dc * T + 512 * tc:dc * T + 512 * (tc + 1)],
                     self.sfm[:, 0:512], r=["g_sfm"], w=["scrD"])
                K.st(self.scrD[:, ND * T + dc * T + 512 * tc:
                                ND * T + dc * T + 512 * (tc + 1)],
                     self.sfm2[:, 0:512], r=["g_sfm2"], w=["scrD"])

        for t in range(NT):
            K.ld(cm3(gth[:, :], ND),
                 cm3(self.scrD[:, 0:ND * T], ND)[:, :, 128 * t:128 * (t + 1)],
                 r=["scrD"], w=["a0_slh"])
            K.ld(cm3(gtl[:, :], ND),
                 cm3(self.scrD[:, ND * T:2 * ND * T], ND)[:, :, 128 * t:128 * (t + 1)],
                 r=["scrD"], w=["a0_sll"])
            self.tr_blocks(gth, ND, lambda c: 128 * c, self.hh, "a0_slh", "g_hh",
                           bank=6)
            self.tr_blocks(gtl, ND, lambda c: 128 * c, self.hl, "a0_sll", "g_hl",
                           bank=7)
            K.v("tensor_tensor", self.xt, self.hh, self.hl, ALU.add,
                r=["g_hh", "g_hl"], w=["g_xt"])
            self.ln_apply(self.xt, self.xn, D, self.epsc[:, 0:1], "g_xt", "g_xn")
            K.v("tensor_tensor", self.xn, self.xn, lnw[:, :], ALU.mult,
                r=["g_xn", "a0_lnw"], w=["g_xn"])
            K.v("tensor_tensor", self.xn, self.xn, lnb[:, :], ALU.add,
                r=["g_xn", "a0_lnb"], w=["g_xn"])
            self.split(self.xn, self.hh, self.hl, "g_xn", "g_hh", "g_hl")
            self.tr_blocks(self.hh, ND, lambda c: 128 * c, self.sfm, "g_hh",
                           "g_sfm", bank=6)
            self.tr_blocks(self.hl, ND, lambda c: 128 * c, self.sfm2, "g_hl",
                           "g_sfm2", bank=7)
            K.st(cm3(self.scrX2[:, 0:ND * T], ND)[:, :, 128 * t:128 * (t + 1)],
                 cm3(self.sfm[:, :], ND), r=["g_sfm"], w=["scrX2"])
            K.st(cm3(self.scrX2[:, ND * T:2 * ND * T], ND)[:, :, 128 * t:128 * (t + 1)],
                 cm3(self.sfm2[:, :], ND), r=["g_sfm2"], w=["scrX2"])

        for tc in range(4):
            self.ld_slice(self.scrX2, "scrX2", tc, slh, sll, "a0_sl")
            psk = self.PS[0:64, 1024:1536]
            psv = self.PS[0:64, 1536:2048]
            for c in range(ND):
                rh = slh[:, 512 * c:512 * (c + 1)]
                rl = sll[:, 512 * c:512 * (c + 1)]
                for ps, of, tag in ((psk, 0, "ps2"), (psv, 64, "ps3")):
                    lh = kvh[:, 128 * c + of:128 * c + of + 64]
                    ll = kvl[:, 128 * c + of:128 * c + of + 64]
                    K.t("matmul", ps, lh, rh, start=(c == 0), stop=False,
                        r=["a0_kvh", "a0_slh"], w=[tag])
                    K.t("matmul", ps, lh, rl, start=False, stop=False,
                        r=["a0_kvh", "a0_sll"], w=[tag])
                    K.t("matmul", ps, ll, rh, start=False, stop=(c == ND - 1),
                        r=["a0_kvl", "a0_slh"], w=[tag])
            K.v("tensor_copy", kT[:, 512 * tc:512 * (tc + 1)], psk,
                r=["ps2"], w=["a0_wb"])
            K.v("tensor_copy", vT[:, 512 * tc:512 * (tc + 1)], psv,
                r=["ps3"], w=["a0_lnb"])
        K.s("activation", wkv[:, :], kT[:, :], AF.Sigmoid, r=["a0_wb"],
            w=["a0_wkv"])
        K.s("activation", wkv[:, :], wkv[:, :], AF.Exp, scale=tfc[:, 0:1],
            r=["a0_wkv", "a0_tfc"], w=["a0_wkv"])
        K.v("tensor_tensor", wkv[:, :], wkv[:, :], vT[:, :], ALU.mult,
            r=["a0_wkv", "a0_lnb"], w=["a0_wkv"])
        K.v("tensor_tensor_scan", states[:, :], tdc[:, 0:1].broadcast_to((64, T)),
            wkv[:, :], 0.0, ALU.mult, ALU.add, r=["a0_wkv", "a0_tdc"], w=["a0_st"])
        self.split(states[:, :], sth[:, :], stl[:, :], "a0_st", "a0_sth", "a0_stl")
        for t in range(NT):
            for d4 in range(4):
                ps = self.PS[:, 512 * d4:512 * (d4 + 1)]
                lh = sth[:, 128 * t:128 * (t + 1)]
                ll = stl[:, 128 * t:128 * (t + 1)]
                rh = woh[:, 512 * d4:512 * (d4 + 1)]
                rl = wol[:, 512 * d4:512 * (d4 + 1)]
                K.t("matmul", ps, lh, rh, start=True, stop=False,
                    r=["a0_sth", "a0_woh"], w=[f"ps{d4}"])
                K.t("matmul", ps, lh, rl, start=False, stop=False,
                    r=["a0_sth", "a0_wol"], w=[f"ps{d4}"])
                K.t("matmul", ps, ll, rh, start=False, stop=True,
                    r=["a0_stl", "a0_woh"], w=[f"ps{d4}"])
            self.finish_tile(t, self.PS[:, 0:2048], ["ps0", "ps1", "ps2", "ps3"],
                             x_src, x_dst)
        K.release(mk)

    def l0_ffn(self, x_src, x_dst):
        K, din = self.K, self.din
        mk = K.mark()
        qT = K.salloc("f0_qT", [128, ND * D], BF16)
        wb = K.salloc("f0_wb", [128, D], F32)
        thr = K.salloc("f0_thr", [128, 16], F32)
        thrT = K.salloc("f0_thrT", [16, 128], F32)
        trow = K.salloc("f0_trow", [1, D], F32)
        br = K.salloc("f0_br", [128, D], F32)
        gth = K.salloc("f0_gth", [128, ND * 128], BF16)
        gtl = K.salloc("f0_gtl", [128, ND * 128], BF16)
        thb, nthb = self.ot, self.ot2
        s1b, s2b = self.hh, self.hl

        K.ld(br[:, :], din["L0_fbr"][:, :], w=["f0_br"])
        for c in range(ND):
            K.ld(wb[:, :], din["L0_fW"][:, D * c:D * (c + 1)], w=["f0_wb"])
            K.v("tensor_reduce", thr[:, c:c + 1], wb[:, :], AX.X, ALU.add,
                apply_absolute_value=True, r=["f0_wb"], w=["f0_thr"])
        K.v("tensor_scalar", thr[:, :], thr[:, :], 0.7 / D, None, ALU.mult,
            r=["f0_thr"], w=["f0_thr"])
        K.t("transpose", self.PS[0:16, 2560:2688], thr[:, :], self.identf[:, :],
            r=["f0_thr", "identf"], w=["ps5"])
        K.v("tensor_copy", thrT[:, :], self.PS[0:16, 2560:2688], r=["ps5"],
            w=["f0_thrT"])
        K.st(trow[0:1, :], thrT[:, :], r=["f0_thrT"], w=["f0_trow"])
        for j in range(4):
            ps = self.PS[:, 2560:3072]
            K.t("matmul", ps, self.oner[:, :], trow[0:1, 512 * j:512 * (j + 1)],
                start=True, stop=True, r=["f0_trow", "oner"], w=["ps5"])
            K.v("tensor_copy", thb[:, 512 * j:512 * (j + 1)], ps, r=["ps5"],
                w=["f0_thb", "g_ot"])
        K.v("tensor_scalar", nthb[:, :], thb[:, :], -1.0, None, ALU.mult,
            r=["f0_thb"], w=["f0_nthb", "g_ot2"])
        for c in range(ND):
            K.ld(wb[:, :], din["L0_fWT"][:, D * c:D * (c + 1)], w=["f0_wb"])
            K.v("tensor_tensor", s1b[:, :], wb[:, :], thb[:, :], ALU.is_gt,
                r=["f0_wb", "f0_thb"], w=["f0_s1", "g_hh"])
            K.v("tensor_tensor", s2b[:, :], wb[:, :], nthb[:, :], ALU.is_lt,
                r=["f0_wb", "f0_nthb"], w=["f0_s2", "g_hl"])
            K.v("tensor_tensor", qT[:, D * c:D * (c + 1)], s1b[:, :], s2b[:, :],
                ALU.subtract, r=["f0_s1", "f0_s2"], w=["f0_qT"])
        self.stage_xT(x_src, self.scrX, "scrX")
        for t in range(NT):
            K.ld(cm3(gth[:, :], ND),
                 cm3(self.scrX[:, 0:ND * T], ND)[:, :, 128 * t:128 * (t + 1)],
                 r=["scrX"], w=["f0_gth"])
            K.ld(cm3(gtl[:, :], ND),
                 cm3(self.scrX[:, ND * T:2 * ND * T], ND)[:, :, 128 * t:128 * (t + 1)],
                 r=["scrX"], w=["f0_gtl"])
            for d4 in range(4):
                ps = self.PS[:, 512 * d4:512 * (d4 + 1)]
                for c in range(ND):
                    lh = gth[:, 128 * c:128 * (c + 1)]
                    ll = gtl[:, 128 * c:128 * (c + 1)]
                    rq = qT[:, D * c + 512 * d4:D * c + 512 * (d4 + 1)]
                    K.t("matmul", ps, lh, rq, start=(c == 0), stop=False,
                        r=["f0_gth", "f0_qT"], w=[f"ps{d4}"])
                    K.t("matmul", ps, ll, rq, start=False, stop=(c == ND - 1),
                        r=["f0_gtl", "f0_qT"], w=[f"ps{d4}"])
            self.finish_tile(t, self.PS[:, 0:2048], ["ps0", "ps1", "ps2", "ps3"],
                             x_src, x_dst, bias_row=br[:, :], r_bias="f0_br")
        K.release(mk)

    def sdpa_head(self, dh, Qh, Ql, rq, Kh_, Kl_, rk, vsl, rv, scale,
                  Sbuf, Pbuf, PTh, PTl, aob, aot_sink):
        K = self.K
        st = self.stats
        for qc in range(4):
            nkt = 4 * (qc + 1)
            for qt4 in range(4):
                for kt in range(4 * qc + qt4 + 1, nkt):
                    K.v("memset", PTh[:, kt * 512 + 128 * qt4:kt * 512 + 128 * (qt4 + 1)],
                        0.0, w=["sdp_PT"])
                    K.v("memset", PTl[:, kt * 512 + 128 * qt4:kt * 512 + 128 * (qt4 + 1)],
                        0.0, w=["sdp_PT"])
            for qt4 in range(4):
                qt = 4 * qc + qt4
                kext = 128 * (qt + 1)
                for kc in range((kext + 511) // 512):
                    ke = min(512, kext - 512 * kc)
                    pb = kc % 2
                    ps = self.PS[:, 512 * pb:512 * pb + ke]
                    lq_h = Qh[:, 128 * qt:128 * (qt + 1)]
                    lq_l = Ql[:, 128 * qt:128 * (qt + 1)]
                    rk_h = Kh_[:, 512 * kc:512 * kc + ke]
                    rk_l = Kl_[:, 512 * kc:512 * kc + ke]
                    K.t("matmul", ps, lq_h, rk_h, start=True, stop=False,
                        r=[rq, rk], w=[f"ps{pb}"])
                    K.t("matmul", ps, lq_h, rk_l, start=False, stop=False,
                        r=[rq, rk], w=[f"ps{pb}"])
                    K.t("matmul", ps, lq_l, rk_h, start=False, stop=True,
                        r=[rq, rk], w=[f"ps{pb}"])
                    nd = ke - 128 if (512 * kc + ke == kext) else ke
                    if nd > 0:
                        K.v("tensor_copy", Sbuf[:, 512 * kc:512 * kc + nd],
                            ps[:, 0:nd], r=[f"ps{pb}"], w=["sdp_S"])
                    if nd < ke:
                        K.v("tensor_tensor", Sbuf[:, 512 * kc + nd:512 * kc + ke],
                            ps[:, nd:ke], self.mask[:, :], ALU.add,
                            r=[f"ps{pb}", "mask"], w=["sdp_S"])
                K.v("reduce_max", st[:, 6:7], Sbuf[:, 0:kext], axis=AX.X,
                    negate=True, r=["sdp_S"], w=["st"])
                K.v("tensor_scalar", st[:, 7:8], st[:, 6:7], scale, None, ALU.mult,
                    r=["st"], w=["st"])
                K.s("activation", Pbuf[:, 0:kext], Sbuf[:, 0:kext], AF.Exp,
                    bias=st[:, 7:8], scale=scale, accum_out=st[:, 8:9],
                    r=["sdp_S", "st"], w=["sdp_P", "st"])
                K.v("reciprocal", self.tcol[:, 0:1], st[:, 8:9], r=["st"],
                    w=["g_tcol"])
                K.v("tensor_scalar", self.hh[:, 0:kext], Pbuf[:, 0:kext],
                    self.tcol[:, 0:1], None, ALU.mult,
                    r=["sdp_P", "g_tcol"], w=["g_hh"])
                K.v("scalar_tensor_tensor", self.hl[:, 0:kext], Pbuf[:, 0:kext],
                    self.tcol[:, 0:1], self.hh[:, 0:kext], ALU.mult, ALU.subtract,
                    r=["sdp_P", "g_tcol", "g_hh"], w=["g_hl"])
                self.tr_blocks(self.hh, qt + 1, lambda kt: kt * 512 + 128 * qt4,
                               PTh, "g_hh", "sdp_PT", bank=6)
                self.tr_blocks(self.hl, qt + 1, lambda kt: kt * 512 + 128 * qt4,
                               PTl, "g_hl", "sdp_PT", bank=7)
            pb = 2 + qc % 2
            ps = self.PS[0:dh, 512 * pb:512 * (pb + 1)]
            for kt in range(nkt):
                vh, vl = vsl(kt)
                pth = PTh[:, kt * 512:(kt + 1) * 512]
                ptl = PTl[:, kt * 512:(kt + 1) * 512]
                K.t("matmul", ps, vh, pth, start=(kt == 0), stop=False,
                    r=[rv, "sdp_PT"], w=[f"ps{pb}"])
                K.t("matmul", ps, vh, ptl, start=False, stop=False,
                    r=[rv, "sdp_PT"], w=[f"ps{pb}"])
                K.t("matmul", ps, vl, pth, start=False, stop=(kt == nkt - 1),
                    r=[rv, "sdp_PT"], w=[f"ps{pb}"])
            K.v("tensor_copy", aob[0:dh, 0:512], ps, r=[f"ps{pb}"], w=["sdp_ao"])
            K.v("tensor_tensor", aob[0:dh, 512:1024], ps, aob[0:dh, 0:512],
                ALU.subtract, r=[f"ps{pb}", "sdp_ao"], w=["sdp_ao"])
            aot_sink(qc)

    def mha(self, x_src, x_dst, layer):
        K, din = self.K, self.din
        mk = K.mark()
        p = f"m{layer}"
        slh = K.salloc(p + "_slh", [128, ND * 512], BF16)
        sll = K.salloc(p + "_sll", [128, ND * 512], BF16)
        wf = K.salloc(p + "_wf", [128, D], F32)
        wqkvh = K.salloc(p + "_wqkvh", [128, 3 * D], BF16)
        wqkvl = K.salloc(p + "_wqkvl", [128, 3 * D], BF16)
        ev = K.salloc(p + "_ev", [128, 1024], BF16)
        qkb = K.salloc(p + "_qkb", [128, 48], F32)
        Qh = K.salloc(p + "_Qh", [128, T], BF16)
        Ql = K.salloc(p + "_Ql", [128, T], BF16)
        Kh_ = K.salloc(p + "_Kh", [128, T], BF16)
        Kl_ = K.salloc(p + "_Kl", [128, T], BF16)
        Vh = K.salloc(p + "_Vh", [128, T], BF16)
        Vl = K.salloc(p + "_Vl", [128, T], BF16)
        Sbuf = K.salloc(p + "_S", [128, T], F32)
        Pbuf = K.salloc(p + "_P", [128, T], F32)
        PTh = K.salloc(p + "_PTh", [128, NT * 512], BF16)
        PTl = K.salloc(p + "_PTl", [128, NT * 512], BF16)
        aob = K.salloc(p + "_aob", [128, 1024], BF16)

        if layer == 2:
            K.ld(qkb[:, :], din["L2_qkb"][:, :], w=[p + "_qkb"])
        woT = din["L2_outT"] if layer == 2 else din["L3_WoT"]
        for c in range(ND):
            K.ld(wf[:, :], woT[:, D * c:D * (c + 1)], w=[p + "_wf"])
            self.split(wf[:, :], self.hh, self.hl, p + "_wf", "g_hh", "g_hl")
            K.st(self.scrX2[:, D * c:D * (c + 1)], self.hh, r=["g_hh"],
                 w=["scrX2"])
            K.st(self.scrX2[:, ND * T + D * c:ND * T + D * (c + 1)], self.hl,
                 r=["g_hl"], w=["scrX2"])
        self.stage_xT(x_src, self.scrX, "scrX")
        scale = 1.0 / np.sqrt(128.0)
        QOF, KOF, VOF, AOF = 0, 2 * 16 * T, 4 * 16 * T, 6 * 16 * T

        for hg in range(16):
            heads = (hg,)
            for hi_, h in enumerate(heads):
                for ki, kind in enumerate(("q", "k", "v")):
                    if layer == 2:
                        wsrc, base = din["L2_qkvT"], {"q": 0, "k": 2048, "v": 4096}[kind] + 128 * h
                    else:
                        wsrc = din[{"q": "L3_WqT", "k": "L3_WkT", "v": "L3_WvT"}[kind]]
                        base = 128 * h
                    K.ld(cm3(wf[:, :], ND), cm3(wsrc[:, :], ND)[:, :, base:base + 128],
                         w=[p + "_wf"])
                    woff = (3 * hi_ + ki) * D
                    self.split(wf[:, :], wqkvh[:, woff:woff + D],
                               wqkvl[:, woff:woff + D], p + "_wf",
                               p + "_wqkvh", p + "_wqkvl")
            for tc in range(4):
                self.ld_slice(self.scrX, "scrX", tc, slh, sll, p + "_sl")
                for hi_, h in enumerate(heads):
                    for ki, kind in enumerate(("q", "k", "v")):
                        woff = (3 * hi_ + ki) * D
                        pb = (3 * hi_ + ki) % 6
                        ps = self.PS[:, 512 * pb:512 * (pb + 1)]
                        for c in range(ND):
                            lh = wqkvh[:, woff + 128 * c:woff + 128 * (c + 1)]
                            ll = wqkvl[:, woff + 128 * c:woff + 128 * (c + 1)]
                            rh = slh[:, 512 * c:512 * (c + 1)]
                            rl = sll[:, 512 * c:512 * (c + 1)]
                            K.t("matmul", ps, lh, rh, start=(c == 0), stop=False,
                                r=[p + "_wqkvh", p + "_slh"], w=[f"ps{pb}"])
                            K.t("matmul", ps, lh, rl, start=False, stop=False,
                                r=[p + "_wqkvh", p + "_sll"], w=[f"ps{pb}"])
                            K.t("matmul", ps, ll, rh, start=False,
                                stop=(c == ND - 1),
                                r=[p + "_wqkvl", p + "_slh"], w=[f"ps{pb}"])
                        if layer == 2:
                            bcol = qkb[:, 16 * ki + h:16 * ki + h + 1]
                            K.v("tensor_scalar", ev[:, 0:512], ps, bcol, None,
                                ALU.add, r=[f"ps{pb}", p + "_qkb"], w=[p + "_ev"])
                            K.v("scalar_tensor_tensor", ev[:, 512:1024], ps, bcol,
                                ev[:, 0:512], ALU.add, ALU.subtract,
                                r=[f"ps{pb}", p + "_qkb", p + "_ev"], w=[p + "_ev"])
                        else:
                            K.v("tensor_copy", ev[:, 0:512], ps, r=[f"ps{pb}"],
                                w=[p + "_ev"])
                            K.v("tensor_tensor", ev[:, 512:1024], ps, ev[:, 0:512],
                                ALU.subtract, r=[f"ps{pb}", p + "_ev"], w=[p + "_ev"])
                        if kind != "v":
                            base_of = QOF if kind == "q" else KOF
                            dst_h = base_of + (2 * h) * T + 512 * tc
                            dst_l = base_of + (2 * h + 1) * T + 512 * tc
                            K.st(self.scrD[:, dst_h:dst_h + 512], ev[:, 0:512],
                                 r=[p + "_ev"], w=["scrD"])
                            K.st(self.scrD[:, dst_l:dst_l + 512], ev[:, 512:1024],
                                 r=[p + "_ev"], w=["scrD"])
                        else:
                            self.tr_blocks(ev[:, 0:512], 4,
                                           lambda j: 128 * j, self.sfm,
                                           p + "_ev", "g_sfm", bank=6)
                            self.tr_blocks(ev[:, 512:1024], 4,
                                           lambda j: 128 * j, self.sfm2,
                                           p + "_ev", "g_sfm2", bank=7)
                            for j in range(4):
                                kt = 4 * tc + j
                                dst_h = VOF + (2 * h) * T + 128 * kt
                                dst_l = VOF + (2 * h + 1) * T + 128 * kt
                                K.st(self.scrD[:, dst_h:dst_h + 128],
                                     self.sfm[:, 128 * j:128 * (j + 1)],
                                     r=["g_sfm"], w=["scrD"])
                                K.st(self.scrD[:, dst_l:dst_l + 128],
                                     self.sfm2[:, 128 * j:128 * (j + 1)],
                                     r=["g_sfm2"], w=["scrD"])

        for h in range(16):
            K.ld(Qh[:, :], self.scrD[:, QOF + 2 * h * T:QOF + (2 * h + 1) * T],
                 r=["scrD"], w=[p + "_Q"])
            K.ld(Ql[:, :], self.scrD[:, QOF + (2 * h + 1) * T:QOF + (2 * h + 2) * T],
                 r=["scrD"], w=[p + "_Q"])
            K.ld(Kh_[:, :], self.scrD[:, KOF + 2 * h * T:KOF + (2 * h + 1) * T],
                 r=["scrD"], w=[p + "_K"])
            K.ld(Kl_[:, :], self.scrD[:, KOF + (2 * h + 1) * T:KOF + (2 * h + 2) * T],
                 r=["scrD"], w=[p + "_K"])
            K.ld(Vh[:, :], self.scrD[:, VOF + 2 * h * T:VOF + (2 * h + 1) * T],
                 r=["scrD"], w=[p + "_V"])
            K.ld(Vl[:, :], self.scrD[:, VOF + (2 * h + 1) * T:VOF + (2 * h + 2) * T],
                 r=["scrD"], w=[p + "_V"])

            def vsl(kt):
                return (Vh[:, 128 * kt:128 * (kt + 1)],
                        Vl[:, 128 * kt:128 * (kt + 1)])

            def aot_sink(qc, h=h):
                dst_h = AOF + (2 * h) * T + 512 * qc
                dst_l = AOF + (2 * h + 1) * T + 512 * qc
                K.st(self.scrD[:, dst_h:dst_h + 512], aob[:, 0:512],
                     r=["sdp_ao"], w=["scrD"])
                K.st(self.scrD[:, dst_l:dst_l + 512], aob[:, 512:1024],
                     r=["sdp_ao"], w=["scrD"])

            self.sdpa_head(128, Qh, Ql, p + "_Q", Kh_, Kl_, p + "_K", vsl,
                           p + "_V", scale, Sbuf, Pbuf, PTh, PTl, aob, aot_sink)

        K.release(mk)
        K.fence()
        wouth = K.salloc(p + "_woh", [128, ND * D], BF16)
        woutl = K.salloc(p + "_wol", [128, ND * D], BF16)
        ah = K.salloc(p + "_ah", [128, ND * 128], BF16)
        al = K.salloc(p + "_al", [128, ND * 128], BF16)
        obr = K.salloc(p + "_obr", [128, D], F32)
        if layer == 2:
            K.ld(obr[:, :], din["L2_obr"][:, :], w=[p + "_obr"])
        for j in range(4):
            K.ld(wouth[:, 8192 * j:8192 * (j + 1)],
                 self.scrX2[:, 8192 * j:8192 * (j + 1)], r=["scrX2"],
                 w=[p + "_wo"])
            K.ld(woutl[:, 8192 * j:8192 * (j + 1)],
                 self.scrX2[:, ND * T + 8192 * j:ND * T + 8192 * (j + 1)],
                 r=["scrX2"], w=[p + "_wo"])
        for t in range(NT):
            K.ld(cm3(ah[:, :], 16),
                 cm3(self.scrD[:, AOF:AOF + 32 * T], 32)[:, 0::2, 128 * t:128 * (t + 1)],
                 r=["scrD"], w=[p + "_ah"])
            K.ld(cm3(al[:, :], 16),
                 cm3(self.scrD[:, AOF:AOF + 32 * T], 32)[:, 1::2, 128 * t:128 * (t + 1)],
                 r=["scrD"], w=[p + "_al"])
            for d4 in range(4):
                ps = self.PS[:, 512 * d4:512 * (d4 + 1)]
                for c in range(ND):
                    lh = ah[:, 128 * c:128 * (c + 1)]
                    ll = al[:, 128 * c:128 * (c + 1)]
                    rh = wouth[:, D * c + 512 * d4:D * c + 512 * (d4 + 1)]
                    rw = woutl[:, D * c + 512 * d4:D * c + 512 * (d4 + 1)]
                    K.t("matmul", ps, lh, rh, start=(c == 0), stop=False,
                        r=[p + "_ah", p + "_wo"], w=[f"ps{d4}"])
                    K.t("matmul", ps, lh, rw, start=False, stop=False,
                        r=[p + "_ah", p + "_wo"], w=[f"ps{d4}"])
                    K.t("matmul", ps, ll, rh, start=False, stop=(c == ND - 1),
                        r=[p + "_al", p + "_wo"], w=[f"ps{d4}"])
            self.finish_tile(t, self.PS[:, 0:2048], ["ps0", "ps1", "ps2", "ps3"],
                             x_src, x_dst,
                             bias_row=(obr[:, :] if layer == 2 else None),
                             r_bias=(p + "_obr" if layer == 2 else None))
        K.release(mk)

    def l2_ffn(self, x_src, x_dst):
        K, din = self.K, self.din
        mk = K.mark()
        wn = K.salloc("f2_wn", [128, ND * D], BF16)
        fsr = K.salloc("f2_fsr", [128, D], F32)
        gth = K.salloc("f2_gth", [128, ND * 128], BF16)
        gtl = K.salloc("f2_gtl", [128, ND * 128], BF16)
        t2 = K.salloc("f2_t2", [128, D], F32)
        for j in range(4):
            K.op("gp", "dma_start", dma=True,
                 out=wn[:, 8192 * j:8192 * (j + 1)],
                 in_=din["L2_fW"][:, 8192 * j:8192 * (j + 1)], w=["f2_wn"])
        K.ld(fsr[:, :], din["L2_fsr"][:, :], w=["f2_fsr"])
        self.stage_xT(x_src, self.scrX, "scrX")

        def chain(src, r_src):
            st = self.stats
            K.v("tensor_tensor", self.ot, src, fsr[:, :], ALU.mult,
                r=list(r_src) + ["f2_fsr"], w=["g_ot"])
            K.s("activation", self.sq, self.ot, AF.Square, accum_out=st[:, 6:7],
                r=["g_ot"], w=["st", "g_sq"])
            K.s("activation", st[:, 7:8], st[:, 6:7], AF.Sqrt, r=["st"], w=["st"])
            K.v("tensor_scalar", st[:, 7:8], st[:, 7:8], 1e-7, None, ALU.max,
                r=["st"], w=["st"])
            K.s("activation", st[:, 8:9], st[:, 7:8], AF.Tanh, r=["st"], w=["st"])
            K.v("tensor_scalar", st[:, 9:10], st[:, 7:8], 1e-7, None, ALU.add,
                r=["st"], w=["st"])
            K.v("reciprocal", st[:, 10:11], st[:, 9:10], r=["st"], w=["st"])
            K.v("tensor_tensor", st[:, 9:10], st[:, 8:9], st[:, 10:11], ALU.mult,
                r=["st"], w=["st"])
            K.v("tensor_scalar", t2[:, :], self.ot, st[:, 9:10], None, ALU.mult,
                r=["g_ot", "st"], w=["f2_t2"])
            K.s("activation", self.sq, t2[:, :], AF.Square, accum_out=st[:, 6:7],
                r=["f2_t2"], w=["st", "g_sq"])
            K.s("activation", st[:, 7:8], st[:, 6:7], AF.Sqrt, r=["st"], w=["st"])
            K.v("tensor_scalar", st[:, 7:8], st[:, 7:8], 1e-7, None, ALU.max,
                r=["st"], w=["st"])
            K.v("reciprocal", st[:, 8:9], st[:, 7:8], r=["st"], w=["st"])
            K.v("tensor_scalar", st[:, 8:9], st[:, 8:9], 0.99, 1.0, ALU.mult,
                ALU.min, r=["st"], w=["st"])
            K.v("tensor_scalar", self.ot, t2[:, :], st[:, 8:9], None, ALU.mult,
                r=["f2_t2", "st"], w=["g_ot"])
            return self.ot, ["g_ot"]

        for t in range(NT):
            K.ld(cm3(gth[:, :], ND),
                 cm3(self.scrX[:, 0:ND * T], ND)[:, :, 128 * t:128 * (t + 1)],
                 r=["scrX"], w=["f2_gth"])
            K.ld(cm3(gtl[:, :], ND),
                 cm3(self.scrX[:, ND * T:2 * ND * T], ND)[:, :, 128 * t:128 * (t + 1)],
                 r=["scrX"], w=["f2_gtl"])
            for d4 in range(4):
                ps = self.PS[:, 512 * d4:512 * (d4 + 1)]
                for c in range(ND):
                    lh = gth[:, 128 * c:128 * (c + 1)]
                    ll = gtl[:, 128 * c:128 * (c + 1)]
                    rq = wn[:, D * c + 512 * d4:D * c + 512 * (d4 + 1)]
                    K.t("matmul", ps, lh, rq, start=(c == 0), stop=False,
                        r=["f2_gth", "f2_wn"], w=[f"ps{d4}"])
                    K.t("matmul", ps, ll, rq, start=False, stop=(c == ND - 1),
                        r=["f2_gtl", "f2_wn"], w=[f"ps{d4}"])
            self.finish_tile(t, self.PS[:, 0:2048], ["ps0", "ps1", "ps2", "ps3"],
                             x_src, x_dst, pre=chain)
        K.release(mk)

    def t_ffn(self, x_src, x_dst, layer):
        K, din = self.K, self.din
        mk = K.mark()
        p = f"tf{layer}"
        H = 5120 if layer == 1 else 8192
        NH = H // 128
        NHS = H // 512
        wfc = din["L1_fcWT"] if layer == 1 else din["L3_fcWT"]
        wpj = din["L1_pjWT"] if layer == 1 else din["L3_pjWT"]
        CW = 4096
        gcol = {wi: K.salloc(p + f"_g{wi}", [128, 3], F32) for wi in range(2)}
        mk2 = K.mark()
        wp = [K.salloc(p + f"_wp{i}", [128, CW], F32) for i in range(2)]
        s1b = K.salloc(p + "_s1", [128, CW], BF16)
        qb = K.salloc(p + "_qb", [128, CW], BF16)
        for wi, (wap, tot) in enumerate(((wfc, ND * H), (wpj, NH * D))):
            acc = self.tcol[:, 4 + wi:5 + wi]
            for j in range(tot // CW):
                b = wp[j % 2]
                K.ld(b[:, :], wap[:, CW * j:CW * (j + 1)], w=[p + f"_wp{j % 2}"])
                K.v("tensor_reduce", self.tcol[:, 6:7], b[:, :], AX.X, ALU.add,
                    apply_absolute_value=True, r=[p + f"_wp{j % 2}"], w=["g_tcol"])
                if j == 0:
                    K.v("tensor_copy", acc, self.tcol[:, 6:7], r=["g_tcol"],
                        w=["g_tcol"])
                else:
                    K.v("tensor_tensor", acc, acc, self.tcol[:, 6:7], ALU.add,
                        r=["g_tcol"], w=["g_tcol"])
            self.colsum(acc, "g_tcol", self.t11[0:1, wi:wi + 1], "g_t11")
            K.v("tensor_scalar", self.t11[0:1, wi:wi + 1], self.t11[0:1, wi:wi + 1],
                1.0 / (D * H), None, ALU.mult, r=["g_t11"], w=["g_t11"])
            if layer == 1:
                K.v("tensor_scalar", self.t11[0:1, wi:wi + 1],
                    self.t11[0:1, wi:wi + 1], 1e-5, None, ALU.max,
                    r=["g_t11"], w=["g_t11"])
                K.v("tensor_scalar", self.t11[0:1, 2 + wi:3 + wi],
                    self.t11[0:1, wi:wi + 1], 0.7, None, ALU.mult,
                    r=["g_t11"], w=["g_t11"])
            else:
                K.v("tensor_scalar", self.t11[0:1, 2 + wi:3 + wi],
                    self.t11[0:1, wi:wi + 1], 0.5, 0.5e-6, ALU.mult, ALU.add,
                    r=["g_t11"], w=["g_t11"])
            cg = gcol[wi]
            self.bc_col(self.t11[0:1, wi:wi + 1], "g_t11", cg[:, 0:1], p + f"_g{wi}")
            self.bc_col(self.t11[0:1, 2 + wi:3 + wi], "g_t11", cg[:, 1:2], p + f"_g{wi}")
            K.v("tensor_scalar", cg[:, 2:3], cg[:, 1:2], -1.0, None, ALU.mult,
                r=[p + f"_g{wi}"], w=[p + f"_g{wi}"])
        for wi, (wap, scr, rscr, tot) in enumerate(
                ((wfc, self.scrA, "scrA", ND * H), (wpj, self.scrB, "scrB", NH * D))):
            cg = gcol[wi]
            for j in range(tot // CW):
                b = wp[j % 2]
                K.ld(b[:, :], wap[:, CW * j:CW * (j + 1)], w=[p + f"_wp{j % 2}"])
                K.v("tensor_single_scalar", s1b[:, :], b[:, :], cg[:, 1:2],
                    ALU.is_gt, r=[p + f"_wp{j % 2}", p + f"_g{wi}"], w=[p + "_s1"])
                K.v("tensor_single_scalar", qb[:, :], b[:, :], cg[:, 2:3],
                    ALU.is_lt, r=[p + f"_wp{j % 2}", p + f"_g{wi}"], w=[p + "_qb"])
                K.v("tensor_tensor", qb[:, :], s1b[:, :], qb[:, :], ALU.subtract,
                    r=[p + "_s1", p + "_qb"], w=[p + "_qb"])
                K.st(scr[:, CW * j:CW * (j + 1)], qb[:, :], r=[p + "_qb"], w=[rscr])
        K.release(mk2)
        K.fence()

        def l3_extra(t):
            st = self.stats
            self.ln_apply(self.xn, self.ot, D, self.epsc[:, 1:2], "g_xn", "g_ot")
            K.v("tensor_reduce", st[:, 6:7], self.ot, AX.X, ALU.max,
                apply_absolute_value=True, r=["g_ot"], w=["st"])
            K.v("reciprocal", st[:, 7:8], st[:, 6:7], r=["st"], w=["st"])
            K.v("tensor_scalar", st[:, 7:8], st[:, 7:8], 128.0, None, ALU.mult,
                r=["st"], w=["st"])
            K.v("tensor_scalar", st[:, 8:9], st[:, 6:7], 1.0 / 128.0, None,
                ALU.mult, r=["st"], w=["st"])
            K.v("tensor_scalar", self.xn, self.ot, st[:, 7:8], None, ALU.mult,
                r=["g_ot", "st"], w=["g_xn"])
            K.v("tensor_scalar", self.xn, self.xn, 128.0 - 1e-6, -(128.0 - 1e-6),
                ALU.min, ALU.max, r=["g_xn"], w=["g_xn"])
            K.v("tensor_scalar", self.xn, self.xn, st[:, 8:9], None, ALU.mult,
                r=["g_xn", "st"], w=["g_xn"])

        self.stage_xT(x_src, self.scrX, "scrX", extra=(l3_extra if layer == 3 else None))

        gh = K.salloc(p + "_gh", [128, 2 * H], F32)
        xgh = K.salloc(p + "_xgh", [128, 2 * ND * 128], BF16)
        xgl = K.salloc(p + "_xgl", [128, 2 * ND * 128], BF16)
        tb = [K.salloc(p + f"_tb{i}", [128, 512], BF16) for i in range(2)]
        hgh = K.salloc(p + "_hgh", [128, H], BF16)
        hgl = K.salloc(p + "_hgl", [128, H], BF16)
        outg = K.salloc(p + "_outg", [128, 2 * D], F32)
        sqh = K.salloc(p + "_sqh", [128, H], BF16)
        tmp5 = K.salloc(p + "_tmp5", [128, 512], F32)
        HOF = 0
        for g in range(8):
            tiles = (2 * g, 2 * g + 1)
            for i, t in enumerate(tiles):
                K.ld(cm3(xgh[:, ND * 128 * i:ND * 128 * (i + 1)], ND),
                     cm3(self.scrX[:, 0:ND * T], ND)[:, :, 128 * t:128 * (t + 1)],
                     r=["scrX"], w=[p + "_xgh"])
                K.ld(cm3(xgl[:, ND * 128 * i:ND * 128 * (i + 1)], ND),
                     cm3(self.scrX[:, ND * T:2 * ND * T], ND)[:, :, 128 * t:128 * (t + 1)],
                     r=["scrX"], w=[p + "_xgl"])
            for hs in range(NHS):
                for c in range(ND):
                    b = tb[c % 2]
                    K.ld(b[:, :], self.scrA[:, H * c + 512 * hs:H * c + 512 * (hs + 1)],
                         r=["scrA"], w=[p + f"_tb{c % 2}"])
                    for i in range(2):
                        ps = self.PS[:, 512 * i:512 * (i + 1)]
                        lh = xgh[:, ND * 128 * i + 128 * c:ND * 128 * i + 128 * (c + 1)]
                        ll = xgl[:, ND * 128 * i + 128 * c:ND * 128 * i + 128 * (c + 1)]
                        K.t("matmul", ps, lh, b[:, :], start=(c == 0), stop=False,
                            r=[p + "_xgh", p + f"_tb{c % 2}"], w=[f"ps{i}"])
                        K.t("matmul", ps, ll, b[:, :], start=False,
                            stop=(c == ND - 1),
                            r=[p + "_xgl", p + f"_tb{c % 2}"], w=[f"ps{i}"])
                for i in range(2):
                    ps = self.PS[:, 512 * i:512 * (i + 1)]
                    dst = gh[:, H * i + 512 * hs:H * i + 512 * (hs + 1)]
                    if layer == 3:
                        K.s("activation", dst, ps, AF.Gelu, scale=gcol[0][:, 0:1],
                            r=[f"ps{i}", p + "_g0"], w=[p + "_gh"])
                    else:
                        K.s("activation", tmp5[:, :], ps, AF.Relu,
                            scale=gcol[0][:, 0:1], r=[f"ps{i}", p + "_g0"],
                            w=[p + "_tmp5"])
                        K.s("activation", dst, tmp5[:, :], AF.Square,
                            r=[p + "_tmp5"], w=[p + "_gh"])
            for i, t in enumerate(tiles):
                row = gh[:, H * i:H * (i + 1)]
                if layer == 3:
                    st = self.stats
                    self.ln_apply(row, row, H, self.epsc[:, 1:2], p + "_gh",
                                  p + "_gh", sq=sqh[:, :], w_sq=p + "_sqh")
                    K.v("tensor_reduce", st[:, 6:7], row, AX.X, ALU.max,
                        apply_absolute_value=True, r=[p + "_gh"], w=["st"])
                    K.v("reciprocal", st[:, 7:8], st[:, 6:7], r=["st"], w=["st"])
                    K.v("tensor_scalar", st[:, 7:8], st[:, 7:8], 128.0, None,
                        ALU.mult, r=["st"], w=["st"])
                    K.v("tensor_scalar", st[:, 8:9], st[:, 6:7], 1.0 / 128.0, None,
                        ALU.mult, r=["st"], w=["st"])
                    K.v("tensor_scalar", row, row, st[:, 7:8], None, ALU.mult,
                        r=[p + "_gh", "st"], w=[p + "_gh"])
                    K.v("tensor_scalar", row, row, 128.0 - 1e-6, -(128.0 - 1e-6),
                        ALU.min, ALU.max, r=[p + "_gh"], w=[p + "_gh"])
                    K.v("tensor_scalar", row, row, st[:, 8:9], None, ALU.mult,
                        r=[p + "_gh", "st"], w=[p + "_gh"])
                self.split(row, hgh[:, :], hgl[:, :], p + "_gh", p + "_hgh",
                           p + "_hgl")
                for c0 in range(0, NH, 4):
                    nb = min(4, NH - c0)
                    for j in range(nb):
                        c = c0 + j
                        K.t("transpose", self.PT6[:, 128 * j:128 * (j + 1)],
                            hgh[:, 128 * c:128 * (c + 1)], self.identb[:, :],
                            r=[p + "_hgh", "identb"], w=["pt6"])
                        K.t("transpose", self.PT7[:, 128 * j:128 * (j + 1)],
                            hgl[:, 128 * c:128 * (c + 1)], self.identb[:, :],
                            r=[p + "_hgl", "identb"], w=["pt7"])
                    for j in range(nb):
                        c = c0 + j
                        K.v("tensor_copy", self.sfm[:, 128 * j:128 * (j + 1)],
                            self.PT6[:, 128 * j:128 * (j + 1)], r=["pt6"],
                            w=["g_sfm"])
                        K.s("copy", self.sfm2[:, 128 * j:128 * (j + 1)],
                            self.PT7[:, 128 * j:128 * (j + 1)], r=["pt7"],
                            w=["g_sfm2"])
                    of = HOF + (2 * t) * (NH * 128) + 128 * c0
                    K.st(self.scrD[:, of:of + 128 * nb], self.sfm[:, 0:128 * nb],
                         r=["g_sfm"], w=["scrD"])
                    of2 = HOF + (2 * t + 1) * (NH * 128) + 128 * c0
                    K.st(self.scrD[:, of2:of2 + 128 * nb], self.sfm2[:, 0:128 * nb],
                         r=["g_sfm2"], w=["scrD"])
            for i, t in enumerate(tiles):
                K.ld(hgh[:, :], self.scrD[:, HOF + 2 * t * NH * 128:
                                          HOF + (2 * t + 1) * NH * 128],
                     r=["scrD"], w=[p + "_hgh2" if False else p + "_hgh"])
                K.ld(hgl[:, :], self.scrD[:, HOF + (2 * t + 1) * NH * 128:
                                          HOF + (2 * t + 2) * NH * 128],
                     r=["scrD"], w=[p + "_hgl"])
                for d4 in range(4):
                    pb = 2 + d4 % 2
                    ps = self.PS[:, 512 * pb:512 * (pb + 1)]
                    for c in range(NH):
                        b = tb[c % 2]
                        K.ld(b[:, :], self.scrB[:, D * c + 512 * d4:
                                                D * c + 512 * (d4 + 1)],
                             r=["scrB"], w=[p + f"_tb{c % 2}"])
                        lh = hgh[:, 128 * c:128 * (c + 1)]
                        ll = hgl[:, 128 * c:128 * (c + 1)]
                        K.t("matmul", ps, lh, b[:, :], start=(c == 0), stop=False,
                            r=[p + "_hgh", p + f"_tb{c % 2}"], w=[f"ps{pb}"])
                        K.t("matmul", ps, ll, b[:, :], start=False,
                            stop=(c == NH - 1),
                            r=[p + "_hgl", p + f"_tb{c % 2}"], w=[f"ps{pb}"])
                    K.v("tensor_scalar", outg[:, D * i + 512 * d4:D * i + 512 * (d4 + 1)],
                        ps, gcol[1][:, 0:1], None, ALU.mult,
                        r=[f"ps{pb}", p + "_g1"], w=[p + "_outg"])
            for i, t in enumerate(tiles):
                self.finish_tile(t, outg[:, D * i:D * (i + 1)], [p + "_outg"],
                                 x_src, x_dst)
        K.release(mk)

    def l1_attn(self, x_src, x_dst):
        K, din = self.K, self.din
        mk = K.mark()
        p = "m1"
        QOF, KOF, VOF, AOF = 0, 8 * T, 16 * T, 24 * T
        slh = K.salloc(p + "_slh", [128, ND * 512], BF16)
        sll = K.salloc(p + "_sll", [128, ND * 512], BF16)
        wf = K.salloc(p + "_wf", [128, 2048], F32)
        w1h = K.salloc(p + "_w1h", [128, 16 * 512], BF16)
        w1l = K.salloc(p + "_w1l", [128, 16 * 512], BF16)
        w2h = K.salloc(p + "_w2h", [128, 16 * 128], BF16)
        w2l = K.salloc(p + "_w2l", [128, 16 * 128], BF16)
        ckvh = K.salloc(p + "_ckvh", [128, 4 * T], BF16)
        ckvl = K.salloc(p + "_ckvl", [128, 4 * T], BF16)
        rr = K.salloc(p + "_rr", [128, T], F32)
        racc = K.salloc(p + "_racc", [128, T], F32)
        rtab = K.salloc(p + "_rtab", [128, T], F32)
        rot = K.salloc(p + "_rot", [128, T], F32)
        tmp5 = K.salloc(p + "_tmp5", [128, 512], F32)
        bcols = K.salloc(p + "_bc", [128, 16], F32)
        cvbr = K.salloc(p + "_cvbr", [128, 512], F32)

        K.ld(bcols[:, 0:4], din["L1_CQb"][:, :], w=[p + "_bc"])
        K.ld(bcols[:, 4:7], din["L1_CQCb"][:, :], w=[p + "_bc"])
        K.ld(bcols[:, 7:8], din["L1_CQRb"][:, :], w=[p + "_bc"])
        K.ld(bcols[:, 8:12], din["L1_CKVb"][:, :], w=[p + "_bc"])
        K.ld(bcols[:, 12:15], din["L1_CKCb"][:, :], w=[p + "_bc"])
        K.ld(bcols[:, 15:16], din["L1_CKRb"][:, :], w=[p + "_bc"])
        K.ld(cvbr[:, :], din["L1_CVbr"][:, :], w=[p + "_cvbr"])
        self.stage_xT(x_src, self.scrX, "scrX")

        def loadw(name, cols, dsth, dstl, nch=16):
            for j in range((nch * cols + 2047) // 2048):
                seg = slice(2048 * j, min(2048 * (j + 1), nch * cols))
                n = seg.stop - seg.start
                K.ld(wf[:, 0:n], din[name][:, seg], w=[p + "_wf"])
                self.split(wf[:, 0:n], dsth[:, seg], dstl[:, seg], p + "_wf",
                           p + "_w1h" if dsth is w1h else p + "_w2h",
                           p + "_w1l" if dsth is w1h else p + "_w2l")

        def xproj(dsth, dstl, wdst, nout_ch, bc0, scr, rscr, gelu, wh, wlb, rw,
                  nc_in=ND, chw=512):
            for tc in range(4):
                self.ld_slice(scr, rscr, tc, slh, sll, p + "_sl", nch=nc_in)
                for oc in range(nout_ch):
                    pb = oc % 2
                    ps = self.PS[:, 512 * pb:512 * (pb + 1)]
                    for c in range(nc_in):
                        lh = wh[:, chw * c + 128 * oc:chw * c + 128 * (oc + 1)]
                        ll = wlb[:, chw * c + 128 * oc:chw * c + 128 * (oc + 1)]
                        rh = slh[:, 512 * c:512 * (c + 1)]
                        rl = sll[:, 512 * c:512 * (c + 1)]
                        K.t("matmul", ps, lh, rh, start=(c == 0), stop=False,
                            r=[rw, p + "_slh"], w=[f"ps{pb}"])
                        K.t("matmul", ps, lh, rl, start=False, stop=False,
                            r=[rw, p + "_sll"], w=[f"ps{pb}"])
                        K.t("matmul", ps, ll, rh, start=False, stop=(c == nc_in - 1),
                            r=[rw, p + "_slh"], w=[f"ps{pb}"])
                    bcol = bcols[:, bc0 + oc:bc0 + oc + 1]
                    dst_sl = slice(oc * T + 512 * tc, oc * T + 512 * (tc + 1))
                    if gelu:
                        K.s("activation", tmp5[:, :], ps, AF.Gelu, bias=bcol,
                            r=[f"ps{pb}", p + "_bc"], w=[p + "_tmp5"])
                        self.split(tmp5[:, :], dsth[:, dst_sl], dstl[:, dst_sl],
                                   p + "_tmp5", wdst, wdst)
                    elif dsth is rr:
                        K.v("tensor_scalar", rr[:, 512 * tc:512 * (tc + 1)], ps,
                            bcol, None, ALU.add, r=[f"ps{pb}", p + "_bc"],
                            w=[p + "_rr"])
                    else:
                        K.v("tensor_scalar", dsth[:, dst_sl], ps, bcol, None,
                            ALU.add, r=[f"ps{pb}", p + "_bc"], w=[wdst])
                        K.v("scalar_tensor_tensor", dstl[:, dst_sl], ps, bcol,
                            dsth[:, dst_sl], ALU.add, ALU.subtract,
                            r=[f"ps{pb}", p + "_bc", wdst], w=[wdst])

        def spill512(src, r_src, chi, clo):
            K.v("tensor_copy", self.sfm[:, 0:512], src, r=[r_src], w=["g_sfm"])
            K.v("tensor_tensor", self.sfm2[:, 0:512], src, self.sfm[:, 0:512],
                ALU.subtract, r=[r_src, "g_sfm"], w=["g_sfm2"])
            K.st(self.scrD[:, chi:chi + 512], self.sfm[:, 0:512], r=["g_sfm"],
                 w=["scrD"])
            K.st(self.scrD[:, clo:clo + 512], self.sfm2[:, 0:512], r=["g_sfm2"],
                 w=["scrD"])

        def rope(base):
            K.ld(rot[0:64, :], rr[64:128, :], r=[p + "_rr"], w=[p + "_rot"])
            K.ld(rot[64:128, :], rr[0:64, :], r=[p + "_rr"], w=[p + "_rot"])
            for j, (tabn, srcb, rsrc) in enumerate(
                    (("L1_cosh", rr, p + "_rr"), ("L1_sinh", rot, p + "_rot"),
                     ("L1_cosl", rr, p + "_rr"), ("L1_sinl", rot, p + "_rot"))):
                K.ld(rtab[:, :], din[tabn][:, :], w=[p + "_rtab"])
                if j == 0:
                    K.v("tensor_tensor", racc[:, :], srcb[:, :], rtab[:, :],
                        ALU.mult, r=[rsrc, p + "_rtab"], w=[p + "_racc"])
                else:
                    K.v("tensor_tensor", self.ot, srcb[:, :], rtab[:, :], ALU.mult,
                        r=[rsrc, p + "_rtab"], w=["g_ot"])
                    K.v("tensor_tensor", racc[:, :], racc[:, :], self.ot, ALU.add,
                        r=[p + "_racc", "g_ot"], w=[p + "_racc"])
            for tc in range(4):
                sl = slice(512 * tc, 512 * (tc + 1))
                K.s("activation", tmp5[:, :], racc[:, sl], AF.Gelu,
                    r=[p + "_racc"], w=[p + "_tmp5"])
                spill512(tmp5[:, :], p + "_tmp5", base + 512 * tc,
                         base + T + 512 * tc)

        def cproj(wname, bc0, base):
            loadw(wname, 384, w2h, w2l, nch=4)
            for tc in range(4):
                for oc in range(3):
                    pb = oc % 2
                    ps = self.PS[:, 512 * pb:512 * (pb + 1)]
                    for c in range(4):
                        lh = w2h[:, 384 * c + 128 * oc:384 * c + 128 * (oc + 1)]
                        ll = w2l[:, 384 * c + 128 * oc:384 * c + 128 * (oc + 1)]
                        rh = ckvh[:, c * T + 512 * tc:c * T + 512 * (tc + 1)]
                        rl = ckvl[:, c * T + 512 * tc:c * T + 512 * (tc + 1)]
                        K.t("matmul", ps, lh, rh, start=(c == 0), stop=False,
                            r=[p + "_w2h", p + "_ckv"], w=[f"ps{pb}"])
                        K.t("matmul", ps, lh, rl, start=False, stop=False,
                            r=[p + "_w2h", p + "_ckv"], w=[f"ps{pb}"])
                        K.t("matmul", ps, ll, rh, start=False, stop=(c == 3),
                            r=[p + "_w2l", p + "_ckv"], w=[f"ps{pb}"])
                    K.s("activation", tmp5[:, :], ps, AF.Gelu,
                        bias=bcols[:, bc0 + oc:bc0 + oc + 1],
                        r=[f"ps{pb}", p + "_bc"], w=[p + "_tmp5"])
                    spill512(tmp5[:, :], p + "_tmp5",
                             base + oc * 2 * T + 512 * tc,
                             base + oc * 2 * T + T + 512 * tc)

        loadw("L1_CKVwT", 512, w1h, w1l)
        xproj(ckvh, ckvl, p + "_ckv", 4, 8, self.scrX, "scrX", True, w1h, w1l,
              p + "_w1h")
        loadw("L1_CKRwT", 128, w2h, w2l)
        xproj(rr, None, p + "_rr", 1, 15, self.scrX, "scrX", False, w2h, w2l,
              p + "_w2h", chw=128)
        rope(KOF + 3 * 2 * T)
        cproj("L1_CKCwT", 12, KOF)
        loadw("L1_CVwT", 512, w2h, w2l, nch=4)
        for t in range(NT):
            pb = t % 2
            ps = self.PS[:, 512 * pb:512 * (pb + 1)]
            for c in range(4):
                lh = ckvh[:, c * T + 128 * t:c * T + 128 * (t + 1)]
                ll = ckvl[:, c * T + 128 * t:c * T + 128 * (t + 1)]
                rh = w2h[:, 512 * c:512 * (c + 1)]
                rl = w2l[:, 512 * c:512 * (c + 1)]
                K.t("matmul", ps, lh, rh, start=(c == 0), stop=False,
                    r=[p + "_ckv", p + "_w2h"], w=[f"ps{pb}"])
                K.t("matmul", ps, lh, rl, start=False, stop=False,
                    r=[p + "_ckv", p + "_w2l"], w=[f"ps{pb}"])
                K.t("matmul", ps, ll, rh, start=False, stop=(c == 3),
                    r=[p + "_ckv", p + "_w2h"], w=[f"ps{pb}"])
            K.v("tensor_tensor", tmp5[:, :], ps, cvbr[:, :], ALU.add,
                r=[f"ps{pb}", p + "_cvbr"], w=[p + "_tmp5"])
            spill512(tmp5[:, :], p + "_tmp5", VOF + 512 * t,
                     VOF + 4 * T + 512 * t)
        loadw("L1_CQwT", 512, w1h, w1l)
        xproj(ckvh, ckvl, p + "_ckv", 4, 0, self.scrX, "scrX", True, w1h, w1l,
              p + "_w1h")
        loadw("L1_CQRwT", 128, w2h, w2l)
        xproj(rr, None, p + "_rr", 1, 7, self.scrX, "scrX", False, w2h, w2l,
              p + "_w2h", chw=128)
        rope(QOF + 3 * 2 * T)
        cproj("L1_CQCwT", 4, QOF)

        K.release(mk)
        K.fence()
        Qph = K.salloc(p + "_Qph", [64, T], BF16)
        Qpl = K.salloc(p + "_Qpl", [64, T], BF16)
        Kph = K.salloc(p + "_Kph", [64, T], BF16)
        Kpl = K.salloc(p + "_Kpl", [64, T], BF16)
        Vph = K.salloc(p + "_Vph", [128, NT * 64], BF16)
        Vpl = K.salloc(p + "_Vpl", [128, NT * 64], BF16)
        Sbuf = K.salloc(p + "_S", [128, T], F32)
        Pbuf = K.salloc(p + "_P", [128, T], F32)
        PTh = K.salloc(p + "_PTh", [128, NT * 512], BF16)
        PTl = K.salloc(p + "_PTl", [128, NT * 512], BF16)
        aob = K.salloc(p + "_aob", [128, 1024], BF16)
        scale = 1.0 / np.sqrt(64.0)
        for h in range(8):
            ch = h // 2
            r0 = 64 * (h % 2)
            qb0 = QOF + ch * 2 * T
            kb0 = KOF + ch * 2 * T
            K.ld(Qph[:, :], self.scrD[r0:r0 + 64, qb0:qb0 + T], r=["scrD"],
                 w=[p + "_Qp"])
            K.ld(Qpl[:, :], self.scrD[r0:r0 + 64, qb0 + T:qb0 + 2 * T],
                 r=["scrD"], w=[p + "_Qp"])
            K.ld(Kph[:, :], self.scrD[r0:r0 + 64, kb0:kb0 + T], r=["scrD"],
                 w=[p + "_Kp"])
            K.ld(Kpl[:, :], self.scrD[r0:r0 + 64, kb0 + T:kb0 + 2 * T],
                 r=["scrD"], w=[p + "_Kp"])
            K.ld(cm3(Vph[:, :], NT),
                 cm3(self.scrD[:, VOF:VOF + 4 * T], NT)[:, :, 64 * h:64 * (h + 1)],
                 r=["scrD"], w=[p + "_Vp"])
            K.ld(cm3(Vpl[:, :], NT),
                 cm3(self.scrD[:, VOF + 4 * T:VOF + 8 * T], NT)[:, :, 64 * h:64 * (h + 1)],
                 r=["scrD"], w=[p + "_Vp"])

            def vsl(kt):
                return (Vph[:, 64 * kt:64 * (kt + 1)],
                        Vpl[:, 64 * kt:64 * (kt + 1)])

            def aot_sink(qc, ch=ch, r0=r0):
                ab = AOF + ch * 2 * T
                K.st(self.scrD[r0:r0 + 64, ab + 512 * qc:ab + 512 * (qc + 1)],
                     aob[0:64, 0:512], r=["sdp_ao"], w=["scrD"])
                K.st(self.scrD[r0:r0 + 64, ab + T + 512 * qc:ab + T + 512 * (qc + 1)],
                     aob[0:64, 512:1024], r=["sdp_ao"], w=["scrD"])

            self.sdpa_head(64, Qph, Qpl, p + "_Qp", Kph, Kpl, p + "_Kp", vsl,
                           p + "_Vp", scale, Sbuf, Pbuf, PTh, PTl, aob, aot_sink)

        K.release(mk)
        K.fence()
        wfo = K.salloc(p + "_wfo", [128, D], F32)
        w3h = K.salloc(p + "_w3h", [128, 4 * D], BF16)
        w3l = K.salloc(p + "_w3l", [128, 4 * D], BF16)
        aoth = K.salloc(p + "_aoth", [128, 4 * T], BF16)
        aotl = K.salloc(p + "_aotl", [128, 4 * T], BF16)
        obr = K.salloc(p + "_obr", [128, D], F32)
        for j in range(4):
            K.ld(wfo[:, :], din["L1_OUTwT"][:, D * j:D * (j + 1)],
                 w=[p + "_wfo"])
            self.split(wfo[:, :], w3h[:, D * j:D * (j + 1)],
                       w3l[:, D * j:D * (j + 1)], p + "_wfo", p + "_w3",
                       p + "_w3")
        K.ld(obr[:, :], din["L1_OUTbr"][:, :], w=[p + "_obr"])
        for ch in range(4):
            ab = AOF + ch * 2 * T
            K.ld(aoth[:, ch * T:(ch + 1) * T], self.scrD[:, ab:ab + T],
                 r=["scrD"], w=[p + "_aot"])
            K.ld(aotl[:, ch * T:(ch + 1) * T], self.scrD[:, ab + T:ab + 2 * T],
                 r=["scrD"], w=[p + "_aot"])
        for t in range(NT):
            for d4 in range(4):
                ps = self.PS[:, 512 * d4:512 * (d4 + 1)]
                for c in range(4):
                    lh = aoth[:, c * T + 128 * t:c * T + 128 * (t + 1)]
                    ll = aotl[:, c * T + 128 * t:c * T + 128 * (t + 1)]
                    rh = w3h[:, 2048 * c + 512 * d4:2048 * c + 512 * (d4 + 1)]
                    rl = w3l[:, 2048 * c + 512 * d4:2048 * c + 512 * (d4 + 1)]
                    K.t("matmul", ps, lh, rh, start=(c == 0), stop=False,
                        r=[p + "_aot", p + "_w3"], w=[f"ps{d4}"])
                    K.t("matmul", ps, lh, rl, start=False, stop=False,
                        r=[p + "_aot", p + "_w3"], w=[f"ps{d4}"])
                    K.t("matmul", ps, ll, rh, start=False, stop=(c == 3),
                        r=[p + "_aot", p + "_w3"], w=[f"ps{d4}"])
            self.finish_tile(t, self.PS[:, 0:2048], ["ps0", "ps1", "ps2", "ps3"],
                             x_src, x_dst, bias_row=obr[:, :], r_bias=p + "_obr")
        K.release(mk)

    def build(self):
        K = self.K
        seq = [self.l0_attn, self.l0_ffn, self.l1_attn,
               lambda a, b: self.t_ffn(a, b, 1),
               lambda a, b: self.mha(a, b, 2), self.l2_ffn,
               lambda a, b: self.mha(a, b, 3),
               lambda a, b: self.t_ffn(a, b, 3)]
        src = self.x_in
        w0, w1 = WINDOW
        for i in range(w0, w1):
            K.fence()
            seq[i](src if i == w0 else self.xres, self.xres)
            src = self.xres
        K.fence()
        ybuf = K.salloc("ybuf", [128, D], F32)
        for t in range(NT):
            K.ld(ybuf, src[128 * t:128 * (t + 1), :], r=["xres"], w=["ybuf"])
            K.st(self.y[128 * t:128 * (t + 1), :], ybuf, r=["ybuf"], w=["ydone"])
        K.emit()
        return self.nc


def kernel(**ins):
    shared = host_prep(ins)
    p = Prog()
    p.build()
    x = np.asarray(ins["L0_attn_x"], np.float32)
    in_maps = []
    for c in range(8):
        m = {"x": np.ascontiguousarray(x[c])}
        for k in _SHAPES:
            m[k] = shared[k]
        in_maps.append(m)
    res = run_bass_kernel_spmd(p.nc, in_maps, core_ids=list(range(8)))
    return np.stack([np.asarray(r["y"], np.float32) for r in res.results], axis=0)
